# version 06 (final)
# baseline (speedup 1.0000x reference)
import numpy as np
from contextlib import ExitStack

import concourse.bass as bass
import concourse.mybir as mybir
from concourse.bass_utils import run_bass_kernel_spmd

F32 = mybir.dt.float32
BF16 = mybir.dt.bfloat16
AF = mybir.ActivationFunctionType
ALU = mybir.AluOpType
AX = mybir.AxisListType

ENGINES = ("sync", "tensor", "vector", "scalar", "gpsimd")

DM, DI, NH, DS, HP, NG, DXB = 2048, 4096, 64, 64, 64, 16, 1024
QL, TOK, NT, COLS, HID = 128, 1024, 8, 10304, 5632
EPS_LN, EPS_RMS, EPS_Q = 1e-5, 1e-5, 1e-8

C_TILE_HEADS = [((m // 4) * 8 + m % 4, (m // 4) * 8 + m % 4 + 4) for m in range(32)]
HEAD_POS = {}
for m, (a, b) in enumerate(C_TILE_HEADS):
    HEAD_POS[a] = (m, 0)
    HEAD_POS[b] = (m, 64)


class Sched:
    def __init__(self):
        self.ops = {e: [] for e in ENGINES}
        self.count = {}

    def emit(self, eng, fn, waits=(), inc=None, n=1, tag=""):
        ws = [(s, v) for w in waits if w is not None for s, v in [w] if v > 0]
        tgt = None
        if inc is not None:
            self.count[inc] = self.count.get(inc, 0) + n
            tgt = self.count[inc]
        self.ops[eng].append(dict(fn=fn, waits=tuple(ws), inc=inc, n=n, tag=tag))
        return (inc, tgt) if inc is not None else None

    def now(self, sem):
        return (sem, self.count.get(sem, 0))

    def allnow(self):
        return [self.now(s) for s in list(self.count)]

    def check(self):
        idx = {e: 0 for e in ENGINES}
        vals = {}
        progress = True
        while progress:
            progress = False
            for e in ENGINES:
                while idx[e] < len(self.ops[e]):
                    op = self.ops[e][idx[e]]
                    if all(vals.get(s, 0) >= v for s, v in op["waits"]):
                        if op["inc"]:
                            vals[op["inc"]] = vals.get(op["inc"], 0) + op["n"]
                        idx[e] += 1
                        progress = True
                    else:
                        break
        stuck = {e: f"{idx[e]}/{len(self.ops[e])} next={self.ops[e][idx[e]]['tag']} waits={self.ops[e][idx[e]]['waits']}"
                 for e in ENGINES if idx[e] < len(self.ops[e])}
        if stuck:
            raise RuntimeError(f"deadlock: {stuck}")

    def install(self, block, sems):
        for ename in ENGINES:
            ops = self.ops[ename]

            def body(eng, ops=ops):
                seen = {}
                for op in ops:
                    for s, v in op["waits"]:
                        if seen.get(s, 0) < v:
                            eng.wait_ge(sems[s], v)
                            seen[s] = v
                    inst = op["fn"](eng)
                    if op["inc"] is not None:
                        assert inst is not None, f"{op['tag']} returned None"
                        if isinstance(inst, list):
                            assert op["n"] == 16 * len(inst), \
                                f"{op['tag']}: n={op['n']} != 16*{len(inst)}"
                            for i2 in inst:
                                i2.then_inc(sems[op["inc"]], 16)
                        else:
                            inst.then_inc(sems[op["inc"]], op["n"])

            getattr(block, ename)(body)


class Ring:
    def __init__(self, n):
        self.free = [None] * n
        self.i = 0
        self.n = n

    def next(self):
        s = self.i % self.n
        self.i += 1
        return s, self.free[s]

    def set_free(self, slot, token):
        self.free[slot] = token


class Arena:
    def __init__(self, nc, base, top):
        self.nc = nc
        self.base = (base + 31) // 32 * 32
        self.top = top
        self.placed = []

    def alloc(self, name, shape, dtype, live, at=None, group=None):
        esz = {F32: 4, BF16: 2}[dtype]
        size = int(np.prod(shape[1:])) * esz
        size_al = (size + 31) // 32 * 32
        s, e = live
        if at is None:
            cands = sorted([p for p in self.placed if not (p[3] < s or p[2] > e)],
                           key=lambda p: p[0])
            off = self.base
            for (o2, sz2, s2, e2, n2) in cands:
                if off + size_al <= o2:
                    break
                off = max(off, o2 + sz2)
            assert off + size_al <= self.top, \
                f"arena overflow placing {name} ({size_al}B) at {off}; top={self.top}"
        else:
            off = at
        self.placed.append((off, size_al, s, e, name))
        return self.nc.alloc_sbuf_tensor_at(name, list(shape), dtype, offset=off).ap()

    def offset_of(self, name):
        for (o, sz, s, e, n) in self.placed:
            if n == name:
                return o
        raise KeyError(name)

    def report(self):
        peak = {}
        for ph in range(8):
            peak[ph] = sum(sz for (o, sz, s, e, n) in self.placed if s <= ph <= e)
        return peak


def build_nc(dbg=(), stop_after=None):
    dbg = set(dbg)
    nc = bass.Bass("TRN2", target_bir_lowering=False, detect_race_conditions=False)

    u = nc.dram_tensor("u", [TOK, DM], F32, kind="ExternalInput").ap()
    ucx = nc.dram_tensor("ucx", [TOK, DM], F32, kind="ExternalInput").ap()
    W_in = nc.dram_tensor("W_in", [DM, COLS], F32, kind="ExternalInput").ap()
    conv_w = nc.dram_tensor("conv_w", [6144, 4], F32, kind="ExternalInput").ap()
    conv_b = nc.dram_tensor("conv_b", [6144], F32, kind="ExternalInput").ap()
    dt_bias = nc.dram_tensor("dt_bias", [NH], F32, kind="ExternalInput").ap()
    A_log = nc.dram_tensor("A_log", [NH], F32, kind="ExternalInput").ap()
    Dp = nc.dram_tensor("Dp", [NH], F32, kind="ExternalInput").ap()
    norm_w = nc.dram_tensor("norm_w", [DI], F32, kind="ExternalInput").ap()
    W_out = nc.dram_tensor("W_out", [DI, DM], F32, kind="ExternalInput").ap()
    w1 = nc.dram_tensor("w1", [HID, DM], F32, kind="ExternalInput").ap()
    w2 = nc.dram_tensor("w2", [HID, DM], F32, kind="ExternalInput").ap()
    w3 = nc.dram_tensor("w3", [DM, HID], F32, kind="ExternalInput").ap()
    out = nc.dram_tensor("out", [TOK, DM], F32, kind="ExternalOutput").ap()

    winb = nc.dram_tensor("winb", [DM, COLS], BF16, kind="Internal").ap()
    woutb = nc.dram_tensor("woutb", [DI, DM], BF16, kind="Internal").ap()
    Hq = nc.dram_tensor("Hq", [NT, DS, NH * HP], BF16, kind="Internal").ap()
    decdr = nc.dram_tensor("decdr", [16, NH], F32, kind="Internal").ap()
    thrdr = nc.dram_tensor("thrdr", [1, 8], F32, kind="Internal").ap()
    q1s = nc.dram_tensor("q1s", [HID, DM], BF16, kind="Internal").ap()
    q2s = nc.dram_tensor("q2s", [HID, DM], BF16, kind="Internal").ap()
    q3s = nc.dram_tensor("q3s", [DM, HID], BF16, kind="Internal").ap()

    dbg_out = {}
    S = Sched()
    tok = {}
    A = Arena(nc, nc.sbuf_base, nc.sbuf_top)

    x_res = A.alloc("x_res", [128, NT * DM], BF16, (0, 7))
    ln1T = A.alloc("ln1T", [128, 16 * TOK], BF16, (3, 6))
    ln_ring = A.alloc("ln_ring", [128, 2 * DM], BF16, (0, 7))
    gtrash = A.alloc("gtrash", [128, 2048], BF16, (0, 7))
    stats = A.alloc("stats", [128, 72], F32, (0, 7))
    ones_sb = A.alloc("ones_sb", [128, 128], BF16, (0, 7))
    identb = A.alloc("identb", [128, 128], BF16, (0, 7))
    identf = A.alloc("identf", [128, 128], F32, (0, 7))
    penalty = A.alloc("penalty", [128, 128], F32, (0, 7))
    zeros64 = A.alloc("zeros64", [64, QL], F32, (0, 7))
    eps_sb = A.alloc("eps_sb", [128, 1], F32, (0, 7))
    onesf_sb = A.alloc("onesf_sb", [128, 128], F32, (0, 7))
    dtb_sb = A.alloc("dtb_sb", [64, 1], F32, (0, 7))
    A_sb = A.alloc("A_sb", [64, 1], F32, (0, 7))
    A_e = A.alloc("A_e", [64, 1], F32, (0, 7))
    D_bc = A.alloc("D_bc", [128, NH], F32, (0, 7))
    nw_fm = A.alloc("nw_fm", [128, 32], F32, (0, 7))
    thr_bc = A.alloc("thr_bc", [128, 8], F32, (0, 7))
    thr_p0 = A.alloc("thr_p0", [1, 8], F32, (0, 7))
    gacc = A.alloc("gacc", [128, 112], F32, (0, 7))
    w3x = A.alloc("w3x", [128, 16], F32, (0, 7))
    w3x2 = A.alloc("w3x2", [128, 16], F32, (0, 7))
    gsum16 = A.alloc("gsum16", [128, 4], BF16, (0, 7))
    gsumf = A.alloc("gsumf", [128, 4], F32, (0, 7))
    cw_sb = A.alloc("cw_sb", [128, 48 * 4], F32, (0, 5))
    cb_sb = A.alloc("cb_sb", [128, 48], F32, (0, 5))
    dec_st = A.alloc("dec_st", [64, 16], F32, (0, 5))
    halo_xb = A.alloc("halo_xb", [128, 16 * 3], BF16, (0, 5))
    halo_ln = A.alloc("halo_ln", [128, 16 * 3], BF16, (0, 5))
    ctail = A.alloc("ctail", [128, 32 * 3], BF16, (0, 5))
    bw_st = A.alloc("bw_st", [128, 4 * 64], BF16, (0, 5))
    dt_tm = A.alloc("dt_tm", [128, 512], F32, (0, 5))
    cum_tm = A.alloc("cum_tm", [128, 512], F32, (0, 5))
    ec_tm = A.alloc("ec_tm", [128, 512], F32, (0, 5))
    dtd_tm = A.alloc("dtd_tm", [128, 512], F32, (0, 5))
    cum_rows = A.alloc("cum_rows", [128, 512], F32, (5, 5))
    gm_sb = A.alloc("gm_sb", [128, 4 * 128], BF16, (0, 5))
    xdt_sb = A.alloc("xdt_sb", [128, 4 * 64], BF16, (0, 5))
    y2s_sb = A.alloc("y2s_sb", [128, 4 * 64], BF16, (0, 5))
    yt_sb = A.alloc("yt_sb", [128, 4 * 64], BF16, (0, 5))
    rstd_sb = A.alloc("rstd_sb", [128, 4 * 64], F32, (0, 5))
    wtile = A.alloc("wtile", [128, 3 * 256], BF16, (0, 5))
    sg_sb = A.alloc("sg_sb", [128, 2 * 512], BF16, (6, 7))
    qt_sb = A.alloc("qt_sb", [128, 3 * 512], BF16, (6, 6))
    lncT = A.alloc("lncT", [128, 16 * TOK], BF16, (1, 2))
    cpost = A.alloc("cpost", [128, 32 * 256], BF16, (5, 5))
    sz_sb = A.alloc("sz_sb", [128, 2 * DI], BF16, (5, 5), at=A.placed[-1][0])
    normT = A.alloc("normT", [128, 32 * 256], BF16, (5, 5), at=A.placed[-2][0])
    y_sb = A.alloc("y_sb", [128, 2 * DI], BF16, (5, 5))
    x_tm = A.alloc("x_tm", [128, 8 * TOK], BF16, (2, 5))
    B_tm = A.alloc("B_tm", [128, 8 * TOK], BF16, (2, 4))
    B_fm = A.alloc("B_fm", [128, 8 * TOK], BF16, (4, 5))
    post_r = A.alloc("post_r", [128, 2 * TOK], BF16, (2, 4))
    pre_ring = A.alloc("pre_ring", [128, 3 * 1027], BF16, (2, 5))
    acc_ring = A.alloc("acc_ring", [128, 2 * 1024], BF16, (2, 5))
    dt_fm = A.alloc("dt_fm", [64, TOK], F32, (2, 4))
    cum_fm = A.alloc("cum_fm", [64, TOK], F32, (2, 5))
    dtd_fm = A.alloc("dtd_fm", [64, TOK], F32, (2, 4))
    sp_abs = A.alloc("sp_abs", [64, TOK], F32, (2, 4))
    dec_bc = A.alloc("dec_bc", [64, 512], F32, (2, 4))
    Hst = A.alloc("Hst", [64, NH * HP], F32, (2, 4))
    s_buf = A.alloc("s_buf", [64, NH * HP], BF16, (2, 4))
    gam_buf = A.alloc("gam_buf", [128, 2 * 2048], BF16, (3, 4))
    wob = A.alloc("wob", [128, 4 * 1024], BF16, (5, 5))
    htld = A.alloc("htld", [128, 2048], BF16, (5, 5))
    diff_sb = A.alloc("diff_sb", [128, 512], F32, (5, 5))
    mt_sb = A.alloc("mt_sb", [128, 2 * 512], BF16, (5, 5))
    ostage = A.alloc("ostage", [128, 2048], F32, (5, 7))
    qw_f32 = A.alloc("qw_f32", [128, 2 * 512], F32, (5, 7))
    qa_sb = A.alloc("qa_sb", [128, 2 * 512], BF16, (5, 7))
    hid_sb = A.alloc("hid_sb", [128, 44 * TOK], BF16, (6, 7))
    ffn_o = A.alloc("ffn_o", [128, 8 * 2048], BF16, (7, 7), at=A.offset_of("ln1T"))
    q3sl = A.alloc("q3sl", [128, 2 * 512], BF16, (7, 7))

    peaks = A.report()

    esP = ExitStack()
    lnps = [esP.enter_context(nc.psum_tensor("lnps0", [128, 1024], BF16)),
            esP.enter_context(nc.psum_tensor("lnps1", [128, 1024], BF16))]
    ip_ps = [esP.enter_context(nc.psum_tensor(f"ip_ps{i}", [128, 512], F32)) for i in range(4)]
    s_ps = [esP.enter_context(nc.psum_tensor("s_ps0", [128, 512], F32)),
            esP.enter_context(nc.psum_tensor("s_ps1", [128, 512], F32))]

    def f_c1(eng):
        eng.memset(ones_sb[:], 1.0)
        eng.memset(zeros64[:], 0.0)
        eng.memset(eps_sb[:], EPS_LN)
        eng.memset(onesf_sb[:], 1.0)
        return eng.memset(penalty[:], 0.0)
    t_c1 = S.emit("gpsimd", f_c1, inc="G", tag="c1")

    def f_c2(eng):
        eng.affine_select(out=identf[:], in_=onesf_sb[:], pattern=[[1, 128]],
                          compare_op=ALU.is_equal, fill=0.0, channel_multiplier=-1)
        eng.affine_select(out=identb[:], in_=ones_sb[:], pattern=[[1, 128]],
                          compare_op=ALU.is_equal, fill=0.0, channel_multiplier=-1)
        return eng.affine_select(out=penalty[:], in_=penalty[:], pattern=[[1, 128]],
                                 compare_op=ALU.is_ge, fill=-1e30, channel_multiplier=-1)
    tok["consts"] = S.emit("gpsimd", f_c2, waits=[t_c1], inc="G", tag="consts")

    def f_wcast(eng):
        r = []
        for i in range(4):
            r.append(eng.dma_start(out=winb[i * 512:(i + 1) * 512, :], in_=W_in[i * 512:(i + 1) * 512, :]))
        for i in range(2):
            r.append(eng.dma_start(out=woutb[i * 2048:(i + 1) * 2048, :], in_=W_out[i * 2048:(i + 1) * 2048, :]))
        return r
    tok["wcast"] = S.emit("gpsimd", f_wcast, inc="dWC", n=16 * 6, tag="wcast")

    def f_small(eng):
        r = []
        r.append(eng.dma_start(out=cw_sb[:, 0:64].rearrange("p (t k) -> p t k", k=4), in_=conv_w[0:2048, :].rearrange("(t p) k -> p t k", p=128), allow_slow_non_contiguous=True))
        r.append(eng.dma_start(out=cb_sb[:, 0:16], in_=conv_b[0:2048].rearrange("(t p) -> p t", p=128), allow_slow_non_contiguous=True))
        cwC = conv_w.rearrange("(c p) k -> c p k", p=64)
        cbC = conv_b.rearrange("(c p) -> c p", p=64)
        for m, (ha, hb) in enumerate(C_TILE_HEADS):
            r.append(eng.dma_start(out=cw_sb[0:64, (16 + m) * 4:(17 + m) * 4], in_=cwC[32 + ha, :, :]))
            r.append(eng.dma_start(out=cw_sb[64:128, (16 + m) * 4:(17 + m) * 4], in_=cwC[32 + hb, :, :]))
            r.append(eng.dma_start(out=cb_sb[0:64, 16 + m:17 + m], in_=cbC[32 + ha, :].rearrange("(p o) -> p o", o=1), allow_slow_non_contiguous=True))
            r.append(eng.dma_start(out=cb_sb[64:128, 16 + m:17 + m], in_=cbC[32 + hb, :].rearrange("(p o) -> p o", o=1), allow_slow_non_contiguous=True))
        r.append(eng.dma_start(out=dtb_sb[:], in_=dt_bias.rearrange("(h o) -> h o", o=1), allow_slow_non_contiguous=True))
        r.append(eng.dma_start(out=nw_fm[:], in_=norm_w.rearrange("(t p) -> p t", p=128), allow_slow_non_contiguous=True))
        r.append(eng.dma_start(out=D_bc[:], in_=Dp.rearrange("(o h) -> o h", o=1).broadcast_to([128, NH]), allow_slow_non_contiguous=True))
        r.append(eng.dma_start(out=A_sb[:], in_=A_log.rearrange("(h o) -> h o", o=1), allow_slow_non_contiguous=True))
        return r
    tok["small"] = S.emit("sync", f_small, inc="dSM", n=16 * 134, tag="small")

    t_ax = S.emit("scalar", lambda e: e.activation(out=A_e[:], in_=A_sb[:], func=AF.Exp),
                  waits=[tok["small"]], inc="C", tag="A_exp")
    tok["A"] = S.emit("vector", lambda e: e.tensor_scalar_mul(out=A_sb[:], in0=A_e[:], scalar1=-1.0),
                      waits=[t_ax], inc="V", tag="A_neg")

    gam_ring = Ring(2)

    def emit_gamma(widx):
        tiles = []
        if widx < 2:
            w = [w1, w2][widx]
            for i in range(44):
                tiles.append((w[i * 128:(i + 1) * 128, :], DM, gacc[:, widx * 44 + i:widx * 44 + i + 1]))
        else:
            for i in range(16):
                tiles.append((w3[i * 128:(i + 1) * 128, 0:2048], 2048, gacc[:, 88 + i:89 + i]))
                tiles.append((w3[i * 128:(i + 1) * 128, 2048:4096], 2048, w3x[:, i:i + 1]))
                tiles.append((w3[i * 128:(i + 1) * 128, 4096:5632], 1536, w3x2[:, i:i + 1]))
        ring = gam_ring
        last = None
        for j, (src_, wd, acc) in enumerate(tiles):
            slot, fr = ring.next()
            dstb = gam_buf[:, slot * 2048: slot * 2048 + wd]
            tl = S.emit("gpsimd", (lambda e, d=dstb, s=src_: e.dma_start(out=d, in_=s)),
                        waits=[fr], inc=f"dGA{slot}", n=16, tag=f"g{widx}ld{j}")
            last = S.emit("scalar",
                          (lambda e, d=dstb, a=acc, wd=wd:
                           e.activation(out=gtrash[:, 0:wd], in_=d, func=AF.Abs, accum_out=a)),
                          waits=[tl], inc="C", tag=f"g{widx}abs{j}")
            ring.set_free(slot, last)
        tok[f"gamma{widx}"] = last

    def emit_gamma_reduce(extra_wait):
        t0a = S.emit("vector", lambda e: e.tensor_tensor(
            out=gacc[:, 88:104], in0=gacc[:, 88:104], in1=w3x[:], op=ALU.add),
            waits=[tok["gamma2"]], inc="V", tag="g2adda")
        t0 = S.emit("vector", lambda e: e.tensor_tensor(
            out=gacc[:, 88:104], in0=gacc[:, 88:104], in1=w3x2[:], op=ALU.add),
            waits=[t0a], inc="V", tag="g2add")
        tr1 = S.emit("vector", lambda e: e.reduce_sum(
            out=gsumf[:, 0:1], in_=gacc[:, 0:44], axis=AX.X),
            waits=[tok["gamma0"], tok["gamma1"], t0], inc="V", tag="gredA")
        tr2 = S.emit("vector", lambda e: e.reduce_sum(
            out=gsumf[:, 1:2], in_=gacc[:, 44:88], axis=AX.X),
            waits=[tr1], inc="V", tag="gredB")
        tr3 = S.emit("vector", lambda e: e.reduce_sum(
            out=gsumf[:, 2:3], in_=gacc[:, 88:104], axis=AX.X),
            waits=[tr2], inc="V", tag="gredC")
        t1 = S.emit("vector", lambda e: e.tensor_copy(out=gsum16[:, 0:3], in_=gsumf[:, 0:3]),
                    waits=[tr3], inc="V", tag="gred")
        t3 = S.emit("tensor", lambda e: e.matmul(s_ps[0][0:1, 0:3], ones_sb[:, 0:1], gsum16[:, 0:3],
                                                 start=True, stop=True),
                    waits=[t1, tok["consts"], extra_wait], inc="T", tag="gmm")

        t4a = S.emit("vector", lambda e: e.memset(thr_p0[:], 0.0),
                     waits=[t3], inc="V", tag="gthrA")
        t4b = S.emit("vector", lambda e: e.tensor_scalar(
            out=thr_p0[0:1, 0:3], in0=s_ps[0][0:1, 0:3],
            scalar1=0.5 / (HID * DM), scalar2=0.5 * EPS_Q, op0=ALU.mult, op1=ALU.add),
            waits=[t4a], inc="V", tag="gthrB")
        t4 = S.emit("vector", lambda e: e.tensor_scalar_mul(
            out=thr_p0[0:1, 4:7], in0=thr_p0[0:1, 0:3], scalar1=-1.0),
            waits=[t4b], inc="V", tag="gthr")
        t5 = S.emit("scalar", lambda e: e.dma_start(out=thrdr[:], in_=thr_p0[:]),
                    waits=[t4], inc="dTH", n=16, tag="gthr_st")
        tok["thr"] = S.emit("sync",
                            lambda e: e.dma_start(out=thr_bc[:], in_=thrdr[0:1, :].broadcast_to([128, 8])),
                            waits=[t5], inc="dTB", n=16, tag="gthr_bc")

    lnp_ring = Ring(2)

    def emit_ln_transpose(name, src_tiles, dstT, halo_dst=None, stat_base=0,
                          loader=None, ring_init=None):
        evs = []
        ring = Ring(2)
        if ring_init is not None:
            ring.free = list(ring_init)
        for t in range(len(src_tiles) if src_tiles is not None else 8):
            if loader is not None:
                slot, fr = ring.next()
                ndst = ln_ring[:, slot * DM:(slot + 1) * DM]
                src, rdy = loader(t, ndst, fr)
                fr = None
            else:
                src, rdy, ndst = src_tiles[t]
                slot = None
                fr = None
                if ndst is None:
                    slot, fr = ring.next()
                    ndst = ln_ring[:, slot * DM:(slot + 1) * DM]
            c = stat_base + (t % 8) * 4
            t_red = S.emit("vector", (lambda e, s=src, c=c: e.reduce_sum(out=stats[:, c:c + 1], in_=s, axis=AX.X)),
                           waits=[rdy], inc="V", tag=f"{name}_sum{t}")
            t_sq = S.emit("scalar",
                          (lambda e, s=src, c=c: e.activation(out=gtrash[:, 0:2048], in_=s, func=AF.Square,
                                                              accum_out=stats[:, c + 1:c + 2])),
                          waits=[rdy], inc="C", tag=f"{name}_sq{t}")

            t_v1 = S.emit("vector", (lambda e, c=c: e.tensor_scalar_mul(
                out=stats[:, c + 2:c + 3], in0=stats[:, c:c + 1], scalar1=1.0 / DM)),
                waits=[t_red, t_sq], inc="V", tag=f"{name}_vm{t}")
            t_v2 = S.emit("vector", (lambda e, c=c: e.tensor_tensor(
                out=stats[:, c + 3:c + 4], in0=stats[:, c + 2:c + 3],
                in1=stats[:, c + 2:c + 3], op=ALU.mult)),
                waits=[t_v1], inc="V", tag=f"{name}_vq{t}")
            t_var = S.emit("vector", (lambda e, c=c: e.scalar_tensor_tensor(
                out=stats[:, c + 3:c + 4], in0=stats[:, c + 1:c + 2],
                scalar=1.0 / DM, in1=stats[:, c + 3:c + 4],
                op0=ALU.mult, op1=ALU.subtract)),
                waits=[t_v2], inc="V", tag=f"{name}_var{t}")
            t_sqr = S.emit("scalar", (lambda e, c=c: e.activation(out=stats[:, c + 3:c + 4],
                                                                  in_=stats[:, c + 3:c + 4],
                                                                  func=AF.Sqrt, bias=eps_sb[:, 0:1])),
                           waits=[t_var], inc="C", tag=f"{name}_sqt{t}")

            t_rcp = S.emit("vector", (lambda e, c=c: e.reciprocal(out=stats[:, c + 3:c + 4],
                                                                  in_=stats[:, c + 3:c + 4])),
                           waits=[t_sqr], inc="V", tag=f"{name}_rcp{t}")

            def fn(e, s=src, d=ndst, c=c):
                return e.tensor_scalar(out=d, in0=s, scalar1=stats[:, c + 2:c + 3],
                                       scalar2=stats[:, c + 3:c + 4], op0=ALU.subtract, op1=ALU.mult)
            t_norm = S.emit("vector", fn, waits=[t_rcp, fr], inc="V", tag=f"{name}_nrm{t}")
            tile_evs = []
            for half in range(2):
                pslot, pfr = lnp_ring.next()
                pb = lnps[pslot]
                last = None
                for dd in range(8):
                    d = half * 8 + dd
                    last = S.emit("tensor",
                                  (lambda e, nd=ndst, d=d, dd=dd, pb=pb:
                                   e.transpose(pb[:, dd * 128:(dd + 1) * 128],
                                               nd[:, d * 128:(d + 1) * 128], identb[:])),
                                  waits=[t_norm, tok["consts"], pfr] if dd == 0 else [t_norm],
                                  inc="T", tag=f"{name}_tp{t}_{d}")
                da = dstT[:].rearrange("p (d w) -> p d w", w=1024)[:, half * 8:(half + 1) * 8,
                                                                  t * 128:(t + 1) * 128]

                def fe(e, pb=pb, da=da, t=t, half=half):
                    if halo_dst is not None and t == len(src_tiles) - 1:
                        hd = halo_dst[:].rearrange("p (d w) -> p d w", w=3)[:, half * 8:(half + 1) * 8, :]
                        e.copy(out=hd, in_=pb[:].rearrange("p (d w) -> p d w", w=128)[:, :, 125:128])
                    return e.copy(out=da, in_=pb[:].rearrange("p (d w) -> p d w", w=128))
                ev = S.emit("scalar", fe, waits=[last], inc="C", tag=f"{name}_ev{t}_{half}")
                lnp_ring.set_free(pslot, ev)
                tile_evs.append(ev)
            if slot is not None:
                ring.set_free(slot, tile_evs[-1])
            evs.append(tile_evs[-1])
        return evs[-1], evs

    ipps_ring = Ring(4)

    def emit_dt_half(name, dt_raw_toks, is_own):
        t1 = S.emit("vector", lambda e: e.tensor_scalar_add(out=dt_fm[:], in0=dt_fm[:],
                                                            scalar1=dtb_sb[:, 0:1]),
                    waits=dt_raw_toks + [tok["small"]], inc="V", tag=f"{name}_bias")
        t2 = S.emit("scalar", lambda e: e.activation(out=sp_abs[:], in_=dt_fm[:], func=AF.Abs),
                    waits=[t1], inc="C", tag=f"{name}_abs")
        t3 = S.emit("scalar", lambda e: e.activation(out=sp_abs[:], in_=sp_abs[:], func=AF.Exp, scale=-1.0),
                    waits=[t2], inc="C", tag=f"{name}_exp")
        t4 = S.emit("vector", lambda e: e.tensor_scalar_add(out=sp_abs[:], in0=sp_abs[:], scalar1=1.0),
                    waits=[t3], inc="V", tag=f"{name}_p1")
        t5 = S.emit("scalar", lambda e: e.activation(out=sp_abs[:], in_=sp_abs[:], func=AF.Ln),
                    waits=[t4], inc="C", tag=f"{name}_lg")
        t6 = S.emit("vector", lambda e: e.scalar_tensor_tensor(out=dt_fm[:], in0=dt_fm[:], scalar=0.0,
                                                               in1=sp_abs[:], op0=ALU.max, op1=ALU.add),
                    waits=[t5], inc="V", tag=f"{name}_sp")
        t7 = S.emit("vector", lambda e: e.tensor_scalar_mul(out=cum_fm[:], in0=dt_fm[:],
                                                            scalar1=A_sb[:, 0:1]),
                    waits=[t6, tok["A"]], inc="V", tag=f"{name}_dA")
        t_cum = []
        for c in range(8):
            csl = slice(c * QL, (c + 1) * QL)
            t_cum.append(S.emit("vector",
                                (lambda e, csl=csl: e.tensor_tensor_scan(out=cum_fm[:, csl], data0=cum_fm[:, csl],
                                                                         data1=zeros64[:], initial=0.0,
                                                                         op0=ALU.add, op1=ALU.add)),
                                waits=[t7], inc="V", tag=f"{name}_cum{c}"))
        t_dtd = []
        for c in range(8):
            csl = slice(c * QL, (c + 1) * QL)
            t_dtd.append(S.emit("vector",
                                (lambda e, csl=csl, c=c: e.tensor_scalar(
                                    out=dtd_fm[:, csl], in0=cum_fm[:, csl],
                                    scalar1=cum_fm[:, c * QL + QL - 1: c * QL + QL],
                                    scalar2=-1.0, op0=ALU.subtract, op1=ALU.mult)),
                                waits=[t_cum[c]], inc="V", tag=f"{name}_d2e{c}"))
        t_e = S.emit("scalar", lambda e: e.activation(out=dtd_fm[:], in_=dtd_fm[:], func=AF.Exp),
                     waits=t_dtd, inc="C", tag=f"{name}_dtde")
        t_m = S.emit("vector", lambda e: e.tensor_tensor(out=dtd_fm[:], in0=dtd_fm[:],
                                                         in1=dt_fm[:], op=ALU.mult),
                     waits=[t_e], inc="V", tag=f"{name}_dtdm")
        hidx = 0 if is_own else 1
        t_dl = S.emit("scalar", (lambda e: e.activation(out=dec_st[:, hidx * 8:(hidx + 1) * 8],
                                                        in_=cum_fm[:, QL - 1::QL], func=AF.Exp)),
                      waits=t_cum, inc="C", tag=f"{name}_dec")
        tok[f"dec_{name}"] = t_dl
        res = {"cum_done": t_cum[-1], "dtdm": t_m}

        def tp_batch(src, dst, tag2, dep, extra=()):
            pslot, pfr = ipps_ring.next()
            pb = ip_ps[pslot]
            last = None
            for c in range(8):
                last = S.emit("tensor",
                              (lambda e, c=c, src=src, pb=pb:
                               e.transpose(pb[:, c * 64:(c + 1) * 64],
                                           src[:, c * QL:(c + 1) * QL], identf[0:64, 0:64])),
                              waits=[dep, tok["consts"], pfr] if c == 0 else [dep],
                              inc="T", tag=f"{name}_{tag2}tp{c}")
            ev = S.emit("scalar", (lambda e, dst=dst, pb=pb: e.copy(out=dst, in_=pb[0:128, 0:512])),
                        waits=[last] + list(extra), inc="C", tag=f"{name}_{tag2}ev")
            ipps_ring.set_free(pslot, ev)
            return ev
        res["dtd_tm"] = tp_batch(dtd_fm, dtd_tm[:], "dtd", t_m, extra=[tok.get("bw_last")])
        if is_own:
            res["dt_tm"] = tp_batch(dt_fm, dt_tm[:], "dt", t6)
            res["cum_tm"] = tp_batch(cum_fm, cum_tm[:], "cum", t_cum[-1])
            res["ec"] = S.emit("scalar", lambda e: e.activation(out=ec_tm[:], in_=cum_tm[:], func=AF.Exp),
                               waits=[res["cum_tm"]], inc="C", tag="own_ec")
        tok[f"dtp_{name}"] = res

    def emit_s_scan(name, is_own, tp_done):
        hidx = 0 if is_own else 1
        t_dw = S.emit("sync",
                      (lambda e: e.dma_start(out=decdr[hidx * 8:(hidx + 1) * 8, :].rearrange("c h -> h c"),
                                             in_=dec_st[:, hidx * 8:(hidx + 1) * 8],
                                             allow_slow_non_contiguous=True)),
                      waits=[tok[f"dec_{name}"]], inc="dDC", n=16, tag=f"{name}_decw")
        t_db = S.emit("sync",
                      (lambda e: e.dma_start(
                          out=dec_bc[:],
                          in_=decdr[hidx * 8:(hidx + 1) * 8, :].rearrange("c h -> (c h)")
                          .rearrange("(o x) -> o x", o=1).broadcast_to([64, 512]))),
                      waits=[t_dw, tok.get("decbc_free")], inc="dDC", n=16, tag=f"{name}_decb")
        bw_ring = Ring(4)
        spring = Ring(2)
        last_bw = None
        for c in range(8):
            evs = []
            for q8 in range(8):
                pslot, pfr = spring.next()
                psb = s_ps[pslot]
                mm_last = None
                for hh in range(8):
                    h = q8 * 8 + hh
                    g = h // 4
                    bt, bhalf = g // 2, (g % 2) * 64
                    slot, bfr = bw_ring.next()

                    def fbw(e, c=c, h=h, bt=bt, bhalf=bhalf, slot=slot):
                        src = B_tm[:, c * 1024 + bt * 128 + bhalf: c * 1024 + bt * 128 + bhalf + 64]
                        return e.tensor_scalar_mul(out=bw_st[:, slot * 64:(slot + 1) * 64], in0=src,
                                                   scalar1=dtd_tm[:, c * 64 + h: c * 64 + h + 1])
                    t_bw = S.emit("vector", fbw,
                                  waits=[tp_done, tok[f"dtp_{name}"]["dtd_tm"], bfr], inc="V",
                                  tag=f"{name}_bw{c}_{h}")
                    last_bw = t_bw
                    wts = [t_bw] + ([pfr] if hh == 0 else [])
                    mm_last = S.emit("tensor",
                                     (lambda e, c=c, hh=hh, bt=bt, bhalf=bhalf, slot=slot, psb=psb:
                                      e.matmul(psb[0:64, hh * 64:(hh + 1) * 64],
                                               bw_st[:, slot * 64:(slot + 1) * 64],
                                               x_tm[:, c * 1024 + bt * 128 + bhalf:
                                                    c * 1024 + bt * 128 + bhalf + 64],
                                               start=True, stop=True)),
                                     waits=wts, inc="T", tag=f"{name}_smm{c}_{h}")
                    bw_ring.set_free(slot, mm_last)
                ev = S.emit("scalar",
                            (lambda e, psb=psb, q8=q8:
                             e.copy(out=s_buf[:, q8 * 512:(q8 + 1) * 512], in_=psb[0:64, 0:512])),
                            waits=[mm_last, tok.get(f"sca_{name}{c-1}")], inc="C",
                            tag=f"{name}_sev{c}_{q8}")
                spring.set_free(pslot, ev)
                evs.append(ev)

            def fmul(e, c=c):
                return e.tensor_tensor(
                    out=Hst[:].rearrange("n (h p) -> n h p", p=HP),
                    in0=Hst[:].rearrange("n (h p) -> n h p", p=HP),
                    in1=dec_bc[:, c * NH:(c + 1) * NH].rearrange("n (h o) -> n h o", o=1)
                    .broadcast_to([64, NH, HP]), op=ALU.mult)
            prev_tok = tok.get(f"sca_{name}{c-1}") if c else tok.get("scan_prev")
            mul_waits = [t_db, prev_tok]
            if is_own:
                t_st = S.emit("gpsimd", (lambda e, c=c: e.dma_start(out=Hq[c], in_=Hst[:])),
                              waits=[prev_tok], inc="dHQ", n=16, tag=f"{name}_hts{c}")
                tok[f"htq{c}"] = t_st
                mul_waits.append(t_st)
            t_mul = S.emit("vector", fmul, waits=mul_waits, inc="V", tag=f"{name}_scm{c}")
            t_add = S.emit("vector",
                           (lambda e: e.tensor_tensor(out=Hst[:], in0=Hst[:], in1=s_buf[:], op=ALU.add)),
                           waits=[t_mul] + evs, inc="V", tag=f"{name}_sca{c}")
            tok[f"sca_{name}{c}"] = t_add
        tok["scan_prev"] = tok[f"sca_{name}7"]
        tok["bw_last"] = last_bw
        tok["decbc_free"] = tok[f"sca_{name}7"]

    wt_ring = Ring(3)
    pre_rng = Ring(3)
    conv_ring = Ring(2)
    post_ring = Ring(2)

    def emit_half(half):
        is_own = half == "own"
        lnT = ln1T if is_own else lncT
        lnT_done = tok[f"lnT_done_{half}"]
        col_tiles = [(4096 + i * 128, 128, i) for i in range(8)] + \
                    [(5120 + i * 128, 128, 8 + i) for i in range(8)] + [(10240, 64, 16)]
        dt_evs = []
        savehalo_toks = []
        tp_evs = []
        for gi in range(0, 17, 2):
            grp = col_tiles[gi:gi + 2]
            mm_info = {}
            for k in range(16):
                wslot, wfr = wt_ring.next()
                gw = sum(g[1] for g in grp)
                wbuf = wtile[:, wslot * 256: wslot * 256 + gw]

                def fl(e, grp=grp, wbuf=wbuf, k=k):
                    off = 0
                    r = []
                    for (c0, wd, tid) in grp:
                        r.append(e.dma_start(out=wbuf[:, off:off + wd],
                                             in_=winb[k * 128:(k + 1) * 128, c0:c0 + wd]))
                        off += wd
                    return r
                tl = S.emit("sync", fl, waits=[wfr, tok["wcast"]], inc=f"dW{wslot}", n=16 * len(grp),
                            tag=f"{half}_w{gi}_{k}")
                last = None
                for ci, (c0, wd, tid) in enumerate(grp):
                    off = sum(g[1] for g in grp[:ci])
                    for th in range(2):
                        pi = ci * 2 + th
                        if k == 0:
                            pslot, pfr = ipps_ring.next()
                            mm_info[(pi, "slot")] = pslot
                            wts = [tl, lnT_done, pfr]
                        else:
                            pslot = mm_info[(pi, "slot")]
                            wts = [tl]
                        last = S.emit("tensor",
                                      (lambda e, wbuf=wbuf, off=off, wd=wd, k=k, th=th,
                                       ps=ip_ps[pslot]:
                                       e.matmul(ps[0:wd, 0:512], wbuf[:, off:off + wd],
                                                lnT[:, k * 1024 + th * 512:k * 1024 + (th + 1) * 512],
                                                start=(k == 0), stop=(k == 15))),
                                      waits=wts, inc="T", tag=f"{half}_mm{gi}_{ci}_{th}_{k}")
                        mm_info[(pi, "tok")] = last
                wt_ring.set_free(wslot, last)
            for ci, (c0, wd, tid) in enumerate(grp):
                if tid == 16:
                    for th in range(2):
                        pslot = mm_info[(ci * 2 + th, "slot")]
                        ev = S.emit("scalar",
                                    (lambda e, th=th, ps=ip_ps[pslot]:
                                     e.copy(out=dt_fm[:, th * 512:(th + 1) * 512],
                                            in_=ps[0:64, 0:512])),
                                    waits=[mm_info[(ci * 2 + th, "tok")], tok.get("dtfam_free")],
                                    inc="C", tag=f"{half}_dtev{th}")
                        dt_evs.append(ev)
                        ipps_ring.set_free(pslot, ev)
                    continue
                pslot_pre, pre_fr = pre_rng.next()
                pre = pre_ring[:, pslot_pre * 1027: pslot_pre * 1027 + 1027]
                if is_own:
                    t_h = S.emit("scalar", (lambda e, pre=pre, tid=tid:
                                            e.copy(out=pre[:, 0:3], in_=halo_xb[:, tid * 3:(tid + 1) * 3])),
                                 waits=[tok["halo_xb_done"], pre_fr], inc="C", tag=f"own_hl{tid}")
                else:
                    t_h = S.emit("vector", (lambda e, pre=pre: e.memset(pre[:, 0:3], 0.0)),
                                 waits=[pre_fr], inc="V", tag=f"ctx_hl{tid}")
                ev_toks = [t_h]
                for th in range(2):
                    pslot = mm_info[(ci * 2 + th, "slot")]
                    ev = S.emit("scalar",
                                (lambda e, pre=pre, th=th, ps=ip_ps[pslot]:
                                 e.copy(out=pre[:, 3 + th * 512:3 + (th + 1) * 512],
                                        in_=ps[0:128, 0:512])),
                                waits=[mm_info[(ci * 2 + th, "tok")], t_h], inc="C",
                                tag=f"{half}_ev{tid}_{th}")
                    ipps_ring.set_free(pslot, ev)
                    ev_toks.append(ev)
                if not is_own:
                    t_sh = S.emit("scalar",
                                  (lambda e, pre=pre, tid=tid:
                                   e.copy(out=halo_xb[:, tid * 3:(tid + 1) * 3], in_=pre[:, 1024:1027])),
                                  waits=ev_toks[1:], inc="C", tag=f"ctx_sh{tid}")
                    savehalo_toks.append(t_sh)
                    ev_toks.append(t_sh)
                if is_own and tid >= 8:
                    post = B_fm[:, (tid - 8) * TOK:(tid - 7) * TOK]
                    post_fr = None
                    post_slot = None
                else:
                    post_slot, post_fr = post_ring.next()
                    post = post_r[:, post_slot * TOK:(post_slot + 1) * TOK]
                aslot, afr = conv_ring.next()
                acc = acc_ring[:, aslot * 1024: aslot * 1024 + TOK]

                def fc(e, pre=pre, acc=acc, tid=tid):
                    e.tensor_scalar(out=acc, in0=pre[:, 0:TOK], scalar1=cw_sb[:, tid * 4:tid * 4 + 1],
                                    scalar2=cb_sb[:, tid:tid + 1], op0=ALU.mult, op1=ALU.add)
                    r = None
                    for kk in range(1, 4):
                        r = e.scalar_tensor_tensor(out=acc, in0=pre[:, kk:kk + TOK],
                                                   scalar=cw_sb[:, tid * 4 + kk:tid * 4 + kk + 1],
                                                   in1=acc, op0=ALU.mult, op1=ALU.add)
                    return r
                t_cv = S.emit("vector", fc, waits=ev_toks + [tok["small"], afr], inc="V",
                              tag=f"{half}_cv{tid}")
                pre_rng.set_free(pslot_pre, t_cv)
                t_si = S.emit("scalar", (lambda e, acc=acc, post=post:
                                         e.activation(out=post, in_=acc, func=AF.Silu)),
                              waits=[t_cv, post_fr], inc="C", tag=f"{half}_si{tid}")
                conv_ring.set_free(aslot, t_si)
                pslotL, pfrL = lnp_ring.next()
                pb = lnps[pslotL]
                lastT = None
                for c in range(8):
                    lastT = S.emit("tensor",
                                   (lambda e, post=post, c=c, pb=pb:
                                    e.transpose(pb[:, c * 128:(c + 1) * 128],
                                                post[:, c * 128:(c + 1) * 128], identb[:])),
                                   waits=[t_si, pfrL] if c == 0 else [t_si], inc="T",
                                   tag=f"{half}_ptp{tid}_{c}")
                dstbuf = x_tm if tid < 8 else B_tm
                dcol = tid if tid < 8 else tid - 8

                def fe(e, pb=pb, dstbuf=dstbuf, dcol=dcol):
                    dst = dstbuf[:].rearrange("p (c w) -> p c w", w=1024)[:, :, dcol * 128:(dcol + 1) * 128]
                    return e.copy(out=dst, in_=pb[:].rearrange("p (c w) -> p c w", w=128))
                evT = S.emit("scalar", fe, waits=[lastT, tok.get(f"xbtm_free_{half}")], inc="C",
                             tag=f"{half}_pte{tid}")
                lnp_ring.set_free(pslotL, evT)
                if post_slot is not None:
                    post_ring.set_free(post_slot, evT)
                tp_evs.append(evT)
        if not is_own:
            tok["halo_xb_done"] = savehalo_toks[-1]
        tp_done = S.emit("scalar", lambda e: e.copy(out=gtrash[:, 0:4], in_=gtrash[:, 0:4]),
                         waits=tp_evs, inc="C", tag=f"{half}_tpf")
        emit_dt_half(half, dt_evs, is_own)
        emit_s_scan(half, is_own, tp_done)
        tok["dtfam_free"] = tok[f"dtp_{half}"]["dtdm"]
        tok[f"xbtm_free_{'own' if not is_own else 'ctx'}"] = tok[f"sca_{half}7"] if not is_own else None
        return tp_done

    t_h0 = S.emit("vector", lambda e: e.memset(Hst[:], 0.0), inc="V", tag="H0")
    tok["scan_prev"] = t_h0

    def ctx_loader(t, ndst, fr):
        tl = S.emit("gpsimd", (lambda e, t=t, d=ndst: e.dma_start(out=d, in_=ucx[t * 128:(t + 1) * 128, :])),
                    waits=[fr], inc=f"dUC{t % 2}", n=16, tag=f"uc{t}")
        return ndst, tl
    tok["lnT_done_ctx"], ctx_evs = emit_ln_transpose("lnc", [None] * 8, lncT, halo_dst=halo_ln,
                                                     loader=ctx_loader)

    emit_half("ctx")

    own_tiles = []
    for t in range(8):
        dst = x_res[:, t * DM:(t + 1) * DM]
        tl = S.emit("gpsimd", (lambda e, t=t, d=dst: e.dma_start(out=d, in_=u[t * 128:(t + 1) * 128, :])),
                    waits=[tok.get(f"ufree{t % 2}")], inc=f"dU{t % 2}", n=16, tag=f"u{t}")
        tok[f"ufree{t % 2}"] = tl
        own_tiles.append((dst, tl, None))
    tok["lnT_done_own"], own_evs2 = emit_ln_transpose("ln1", own_tiles, ln1T, stat_base=40,
                                                 ring_init=[tok["lnT_done_ctx"]] * 2)

    emit_gamma(0)
    own_tp_done = emit_half("own")
    emit_gamma(1)
    emit_gamma(2)
    emit_gamma_reduce(extra_wait=tok["sca_own7"])

    def dbg_tap(name, ap, waits):
        if name in dbg:
            d = nc.dram_tensor("dbg_" + name, list(ap.shape), ap.tensor.dtype,
                               kind="ExternalOutput").ap()
            dbg_out[name] = d
            S.emit("sync", (lambda e, d=d, ap=ap: e.dma_start(out=d, in_=ap)), waits=waits,
                   inc="dDB", n=16, tag=f"dbg_{name}")

    a0_done = [tok["sca_own7"], own_tp_done]
    for nm, ap, w in [("stats", stats[:], a0_done), ("xres", x_res[:], a0_done), ("cum", cum_fm[:], a0_done), ("dt", dt_fm[:], a0_done),
                      ("dtd", dtd_fm[:], a0_done), ("H", Hst[:], a0_done),
                      ("ln1T", ln1T[:], [tok["lnT_done_own"]]),
                      ("Bfm", B_fm[:], a0_done), ("xtm", x_tm[:], a0_done),
                      ("Btm", B_tm[:], a0_done),
                      ("dttm", dt_tm[:], a0_done), ("cumtm", cum_tm[:], a0_done),
                      ("ectm", ec_tm[:], a0_done), ("dtdtm", dtd_tm[:], a0_done),
                      ("halox", halo_xb[:], a0_done), ("haloln", halo_ln[:], a0_done)]:
        dbg_tap(nm, ap, w)
    if "Hq" in dbg:
        d = nc.dram_tensor("dbg_Hq", [NT, DS, NH * HP], BF16, kind="ExternalOutput").ap()
        dbg_out["Hq"] = d
        S.emit("sync", lambda e: e.dma_start(out=d, in_=Hq[:]),
               waits=[tok["sca_own7"], S.now("dHQ")], inc="dDB", n=16, tag="dbg_Hq")

    def finish(last_waits):
        try:
            with open("/tmp/arena.txt", "w") as fh:
                for (o, sz, s_, e_, n_) in sorted(A.placed):
                    fh.write(f"{o:7d} {o+sz:7d} ph{s_}-{e_} {n_}\n")
        except OSError:
            pass
        fence_waits = [S.now(s) for s in list(S.count)] + [w for w in last_waits if w]
        S.emit("sync", lambda e: e.nop(), waits=fence_waits, tag="fence")
        used = set()
        for e in ENGINES:
            for op in S.ops[e]:
                if op["inc"]:
                    used.add(op["inc"])
                for s, _ in op["waits"]:
                    used.add(s)
        sems_obj = {s: esP.enter_context(nc.semaphore(s)) for s in sorted(used)}
        S.check()
        with nc.Block() as block:
            S.install(block, sems_obj)
        esP.close()
        return nc, dbg_out

    if stop_after == "A0":
        return finish([tok["thr"]])

    a1_halo_ring = Ring(2)
    a1_res_toks = []

    def a1stop(point, taps):
        if stop_after == point:
            for nm, ap in taps:
                dbg_tap(nm, ap, S.allnow())
            return True
        return False

    def emit_A1_block(b):
        tok0 = 256 * b
        cpost_toks = {}
        for mg in range(0, 32, 2):
            grp = [mg, mg + 1]
            mm_info = {}
            for k in range(16):
                wslot, wfr = wt_ring.next()
                wbuf = wtile[:, wslot * 256: wslot * 256 + 256]

                def fl(e, grp=grp, wbuf=wbuf, k=k):
                    r = []
                    for ci, m in enumerate(grp):
                        ha, hb = C_TILE_HEADS[m]
                        r.append(e.dma_start(out=wbuf[:, ci * 128:ci * 128 + 64],
                                             in_=winb[k * 128:(k + 1) * 128, 6144 + ha * 64:6144 + (ha + 1) * 64]))
                        r.append(e.dma_start(out=wbuf[:, ci * 128 + 64:ci * 128 + 128],
                                             in_=winb[k * 128:(k + 1) * 128, 6144 + hb * 64:6144 + (hb + 1) * 64]))
                    return r
                tl = S.emit("sync", fl, waits=[wfr, tok["wcast"]], inc=f"dW{wslot}", n=16 * 4,
                            tag=f"A1b{b}_cw{mg}_{k}")
                last = None
                for ci, m in enumerate(grp):
                    if k == 0:
                        pslot, pfr = ipps_ring.next()
                        mm_info[(ci, "slot")] = pslot
                        wts = [tl, tok["lnT_done_own"], pfr, tok.get("cpost_ok")]
                    else:
                        pslot = mm_info[(ci, "slot")]
                        wts = [tl]
                    last = S.emit("tensor",
                                  (lambda e, wbuf=wbuf, ci=ci, k=k, ps=ip_ps[pslot]:
                                   e.matmul(ps[0:128, 0:256], wbuf[:, ci * 128:(ci + 1) * 128],
                                            ln1T[:, k * 1024 + tok0: k * 1024 + tok0 + 256],
                                            start=(k == 0), stop=(k == 15))),
                                  waits=wts, inc="T", tag=f"A1b{b}_cmm{mg}_{ci}_{k}")
                    mm_info[(ci, "tok")] = last
                    if b == 0:
                        if k == 0:
                            hslot, hfr = a1_halo_ring.next()
                            mm_info[(ci, "hslot")] = hslot
                            hwts = [tl, tok["lnT_done_ctx"], hfr]
                        else:
                            hslot = mm_info[(ci, "hslot")]
                            hwts = [tl]
                        hlast = S.emit("tensor",
                                       (lambda e, wbuf=wbuf, ci=ci, k=k, hs=s_ps[hslot]:
                                        e.matmul(hs[0:128, 0:3], wbuf[:, ci * 128:(ci + 1) * 128],
                                                 halo_ln[:, k * 3:(k + 1) * 3],
                                                 start=(k == 0), stop=(k == 15))),
                                       waits=hwts, inc="T", tag=f"A1b{b}_hmm{mg}_{ci}_{k}")
                        mm_info[(ci, "htok")] = hlast
                wt_ring.set_free(wslot, last)
            for ci, m in enumerate(grp):
                pslot_pre, pre_fr = pre_rng.next()
                pre = pre_ring[:, pslot_pre * 1027: pslot_pre * 1027 + 259]
                if b == 0:
                    hslot = mm_info[(ci, "hslot")]
                    t_h = S.emit("scalar",
                                 (lambda e, pre=pre, hs=s_ps[hslot]:
                                  e.copy(out=pre[:, 0:3], in_=hs[0:128, 0:3])),
                                 waits=[mm_info[(ci, "htok")], pre_fr], inc="C", tag=f"A1b0_he{m}")
                    a1_halo_ring.set_free(hslot, t_h)
                else:
                    t_h = S.emit("scalar",
                                 (lambda e, pre=pre, m=m:
                                  e.copy(out=pre[:, 0:3], in_=ctail[:, m * 3:(m + 1) * 3])),
                                 waits=[tok[f"ctail_done{b-1}"], pre_fr], inc="C", tag=f"A1b{b}_he{m}")
                pslot = mm_info[(ci, "slot")]
                ev = S.emit("scalar",
                            (lambda e, pre=pre, ps=ip_ps[pslot]:
                             e.copy(out=pre[:, 3:259], in_=ps[0:128, 0:256])),
                            waits=[mm_info[(ci, "tok")], t_h], inc="C", tag=f"A1b{b}_ce{m}")
                ipps_ring.set_free(pslot, ev)
                toks = [t_h, ev]
                if b < 3:
                    t_sh = S.emit("scalar",
                                  (lambda e, pre=pre, m=m:
                                   e.copy(out=ctail[:, m * 3:(m + 1) * 3], in_=pre[:, 256:259])),
                                  waits=[ev], inc="C", tag=f"A1b{b}_ct{m}")
                    toks.append(t_sh)
                    cpost_toks[("ct", m)] = t_sh
                aslot, afr = conv_ring.next()
                acc = acc_ring[:, aslot * 1024: aslot * 1024 + 256]
                cwi = 16 + m

                def fc(e, pre=pre, acc=acc, cwi=cwi):
                    e.tensor_scalar(out=acc, in0=pre[:, 0:256], scalar1=cw_sb[:, cwi * 4:cwi * 4 + 1],
                                    scalar2=cb_sb[:, cwi:cwi + 1], op0=ALU.mult, op1=ALU.add)
                    r = None
                    for kk in range(1, 4):
                        r = e.scalar_tensor_tensor(out=acc, in0=pre[:, kk:kk + 256],
                                                   scalar=cw_sb[:, cwi * 4 + kk:cwi * 4 + kk + 1],
                                                   in1=acc, op0=ALU.mult, op1=ALU.add)
                    return r
                t_cv = S.emit("vector", fc, waits=toks + [tok["small"], afr], inc="V",
                              tag=f"A1b{b}_cv{m}")
                pre_rng.set_free(pslot_pre, t_cv)
                t_si = S.emit("scalar",
                              (lambda e, acc=acc, m=m:
                               e.activation(out=cpost[:, m * 256:(m + 1) * 256], in_=acc, func=AF.Silu)),
                              waits=[t_cv], inc="C", tag=f"A1b{b}_cs{m}")
                conv_ring.set_free(aslot, t_si)
                cpost_toks[m] = t_si
        if b < 3:
            tok[f"ctail_done{b}"] = S.emit("scalar",
                                           lambda e: e.copy(out=gtrash[:, 0:4], in_=gtrash[:, 0:4]),
                                           waits=[v for kk, v in cpost_toks.items()], inc="C",
                                           tag=f"ctail_f{b}")
        if a1stop(f"b{b}conv", [("cpost", cpost[:])]):
            return True
        cb_ring = Ring(2)
        mtr_ring = Ring(2)
        g_ring = Ring(8)
        y_ring = Ring(8)
        y2_ring = Ring(8)
        st4_ring = Ring(4)
        cb_ring.free = [ipps_ring.free[0], ipps_ring.free[1]]
        y_ring.free = [ipps_ring.free[2]] * 8
        y2_ring.free = [ipps_ring.free[3]] * 8
        g_ring.free = [tok.get("gprev_fence")] * 8
        for cl in range(2):
            cg = 2 * b + cl
            hq5 = Hq[cg].rearrange("n (q bh r p) -> n q bh r p", q=8, bh=2, r=4, p=64)

            def fhl(e, hq5=hq5):
                return [e.dma_start(out=htld[0:64, :].rearrange("n (q r p) -> n q r p", q=8, r=4, p=64),
                                    in_=hq5[:, :, 0, :, :]),
                        e.dma_start(out=htld[64:128, :].rearrange("n (q r p) -> n q r p", q=8, r=4, p=64),
                                    in_=hq5[:, :, 1, :, :])]
            t_hl = S.emit("sync", fhl, waits=[tok[f"htq{cg}"], tok.get("htld_free"),
                                              tok["sca_own7"], tok["hq_done"]], inc="dHL", n=32,
                          tag=f"A1b{b}_hl{cl}")
            y2_last = []
            for hb in range(16):
                def fcr(e, cg=cg, hb=hb):
                    return e.dma_start(out=cum_rows[0:1, 0:512],
                                       in_=cum_fm[hb * 4:(hb + 1) * 4, cg * QL:(cg + 1) * QL])
                t_cr = S.emit("sync", fcr, waits=[tok["dtp_own"]["cum_done"], tok.get("cr_free"),
                                                  tok["sca_own7"]],
                              inc="dCR", n=16, tag=f"A1b{b}_cr{cl}_{hb}")
                cbslot, cbfr = cb_ring.next()
                t_cb = S.emit("tensor",
                              (lambda e, ps=ip_ps[cbslot]:
                               e.matmul(ps[0:128, 0:512], onesf_sb[0:1, :],
                                        cum_rows[0:1, 0:512],
                                        start=True, stop=True)),
                              waits=[t_cr, cbfr, tok["consts"]], inc="T", tag=f"A1b{b}_cb{cl}_{hb}")
                tok["cr_free"] = t_cb
                dif_toks = []
                for hh in range(4):
                    h = hb * 4 + hh

                    def fd(e, hh=hh, h=h, cg=cg, ps=ip_ps[cbslot]):
                        return e.scalar_tensor_tensor(
                            out=diff_sb[:, hh * 128:(hh + 1) * 128],
                            in0=ps[0:128, hh * 128:(hh + 1) * 128],
                            scalar=cum_tm[:, cg * 64 + h: cg * 64 + h + 1],
                            in1=penalty[:], op0=ALU.subtract, op1=ALU.add)
                    dif_toks.append(S.emit("vector", fd,
                                           waits=[t_cb, tok["dtp_own"]["cum_tm"],
                                                  tok.get("diff_free")] if hh == 0
                                           else [t_cb], inc="V", tag=f"A1b{b}_df{cl}_{h}"))
                cb_ring.set_free(cbslot, dif_toks[-1])
                mtslot, mtfr = mtr_ring.next()
                t_mt = S.emit("scalar",
                              (lambda e, mtslot=mtslot:
                               e.activation(out=mt_sb[:, mtslot * 512:(mtslot + 1) * 512],
                                            in_=diff_sb[:], func=AF.Exp)),
                              waits=dif_toks + [mtfr], inc="C", tag=f"A1b{b}_mt{cl}_{hb}")
                tok["diff_free"] = t_mt
                gm_toks = []
                for hh in range(4):
                    h = hb * 4 + hh
                    m, base = HEAD_POS[h]
                    g = h // 4
                    bt = g // 2
                    gslot, gfr = g_ring.next()
                    gps = s_ps[gslot // 4][:, (gslot % 4) * 128:(gslot % 4) * 128 + 128]
                    t_g = S.emit("tensor",
                                 (lambda e, bt=bt, base=base, cg=cg, cl=cl, m=m, gps=gps:
                                  e.matmul(gps,
                                           B_fm[base:base + 64, bt * TOK + cg * QL: bt * TOK + (cg + 1) * QL],
                                           cpost[base:base + 64, m * 256 + cl * 128: m * 256 + (cl + 1) * 128],
                                           start=True, stop=True)),
                                 waits=[cpost_toks[m], gfr, own_tp_done], inc="T",
                                 tag=f"A1b{b}_g{cl}_{h}")
                    sslot, sfr = st4_ring.next()
                    t_gm = S.emit("vector",
                                  (lambda e, gps=gps, mtslot=mtslot, hh=hh, sslot=sslot:
                                   e.tensor_tensor(out=gm_sb[:, sslot * 128:(sslot + 1) * 128],
                                                   in0=gps,
                                                   in1=mt_sb[:, mtslot * 512 + hh * 128:
                                                             mtslot * 512 + (hh + 1) * 128],
                                                   op=ALU.mult)),
                                  waits=[t_g, t_mt, sfr], inc="V", tag=f"A1b{b}_gm{cl}_{h}")
                    g_ring.set_free(gslot, t_gm)
                    gm_toks.append(t_gm)
                    t_xd = S.emit("vector",
                                  (lambda e, bt=bt, base=base, cg=cg, h=h, sslot=sslot:
                                   e.tensor_scalar_mul(
                                       out=xdt_sb[:, sslot * 64:(sslot + 1) * 64],
                                       in0=x_tm[:, cg * 1024 + bt * 128 + base:
                                                cg * 1024 + bt * 128 + base + 64],
                                       scalar1=dt_tm[:, cg * 64 + h: cg * 64 + h + 1])),
                                  waits=[tok["dtp_own"]["dt_tm"], t_gm], inc="V",
                                  tag=f"A1b{b}_xd{cl}_{h}")
                    yslot, yfr = y_ring.next()
                    t_yi = S.emit("tensor",
                                  (lambda e, sslot=sslot, yslot=yslot:
                                   e.matmul(ip_ps[2][0:128, yslot * 64:(yslot + 1) * 64],
                                            gm_sb[:, sslot * 128:(sslot + 1) * 128],
                                            xdt_sb[:, sslot * 64:(sslot + 1) * 64],
                                            start=True, stop=True)),
                                  waits=[t_gm, t_xd, yfr], inc="T", tag=f"A1b{b}_yi{cl}_{h}")
                    y2slot, y2fr = y2_ring.next()
                    t_y2 = S.emit("tensor",
                                  (lambda e, base=base, m=m, cl=cl, y2slot=y2slot:
                                   e.matmul(ip_ps[3][0:128, y2slot * 64:(y2slot + 1) * 64],
                                            cpost[base:base + 64, m * 256 + cl * 128:
                                                  m * 256 + (cl + 1) * 128],
                                            htld[base:base + 64, m * 64:(m + 1) * 64],
                                            start=True, stop=True)),
                                  waits=[t_hl, cpost_toks[m], y2fr], inc="T", tag=f"A1b{b}_y2{cl}_{h}")
                    y2_last.append(t_y2)
                    t_y2s = S.emit("scalar",
                                   (lambda e, y2slot=y2slot, cg=cg, h=h, sslot=sslot:
                                    e.activation(out=y2s_sb[:, sslot * 64:(sslot + 1) * 64],
                                                 in_=ip_ps[3][0:128, y2slot * 64:(y2slot + 1) * 64],
                                                 func=AF.Copy,
                                                 scale=ec_tm[:, cg * 64 + h: cg * 64 + h + 1])),
                                   waits=[t_y2, tok["dtp_own"]["ec"]], inc="C",
                                   tag=f"A1b{b}_y2s{cl}_{h}")
                    y2_ring.set_free(y2slot, t_y2s)

                    def fyt(e, bt=bt, base=base, cg=cg, h=h, sslot=sslot):
                        return e.scalar_tensor_tensor(
                            out=yt_sb[:, sslot * 64:(sslot + 1) * 64],
                            in0=x_tm[:, cg * 1024 + bt * 128 + base: cg * 1024 + bt * 128 + base + 64],
                            scalar=D_bc[:, h:h + 1],
                            in1=y2s_sb[:, sslot * 64:(sslot + 1) * 64],
                            op0=ALU.mult, op1=ALU.add)
                    t_yt = S.emit("vector", fyt, waits=[t_y2s, tok["small"]], inc="V",
                                  tag=f"A1b{b}_yt{cl}_{h}")

                    def fyf(e, cl=cl, h=h, sslot=sslot, yslot=yslot):
                        return e.tensor_tensor(out=y_sb[:, cl * DI + h * 64: cl * DI + (h + 1) * 64],
                                               in0=yt_sb[:, sslot * 64:(sslot + 1) * 64],
                                               in1=ip_ps[2][0:128, yslot * 64:(yslot + 1) * 64],
                                               op=ALU.add)
                    t_yf = S.emit("vector", fyf, waits=[t_yt, t_yi, tok.get(f"y_free{cl}")],
                                  inc="V", tag=f"A1b{b}_yf{cl}_{h}")
                    y_ring.set_free(yslot, t_yf)
                    st4_ring.set_free(sslot, t_yf)
                mtr_ring.set_free(mtslot, gm_toks[-1])
            tok["htld_free"] = y2_last[-1]
        tok["ipA_free"] = S.now("V")
        tok["ipB_free"] = S.now("C")
        tok["gprev_fence"] = S.now("V")
        ssd_T_fence = S.now("T")
        ssd_V_fence = S.now("V")
        if a1stop(f"b{b}ssd", [("y", y_sb[:]), ("cpost", cpost[:])]):
            return True
        zw_ring = Ring(4)
        zw_ring.free = [tok.get("zw_free")] * 4
        for zct in range(8):
            mm_info = {}
            for k in range(16):
                wslot, wfr = zw_ring.next()
                wbuf = wob[:, wslot * 512:(wslot + 1) * 512]

                def fl(e, zct=zct, k=k, wbuf=wbuf):
                    return e.dma_start(out=wbuf, in_=winb[k * 128:(k + 1) * 128, zct * 512:(zct + 1) * 512])
                tl = S.emit("sync", fl, waits=[wfr, tok["wcast"], tok.get("wob_zfree"),
                                               tok["sca_own7"], tok["hq_done"]], inc=f"dZ{wslot}", n=16,
                            tag=f"A1b{b}_zw{zct}_{k}")
                last = None
                for tb in range(2):
                    if k == 0:
                        pslot, pfr = ipps_ring.next()
                        mm_info[(tb, "slot")] = pslot
                        wts = [tl, pfr, tok["ipA_free"], tok["ipB_free"]]
                    else:
                        pslot = mm_info[(tb, "slot")]
                        wts = [tl]
                    last = S.emit("tensor",
                                  (lambda e, k=k, tb=tb, wbuf=wbuf, ps=ip_ps[pslot]:
                                   e.matmul(ps[0:128, 0:512],
                                            ln1T[:, k * 1024 + tok0 + tb * 128: k * 1024 + tok0 + (tb + 1) * 128],
                                            wbuf, start=(k == 0), stop=(k == 15))),
                                  waits=wts, inc="T", tag=f"A1b{b}_zmm{zct}_{tb}_{k}")
                    mm_info[(tb, "tok")] = last
                zw_ring.set_free(wslot, last)
            for tb in range(2):
                pslot = mm_info[(tb, "slot")]
                tz = S.emit("scalar",
                            (lambda e, tb=tb, zct=zct, ps=ip_ps[pslot]:
                             e.activation(out=sz_sb[:, tb * DI + zct * 512: tb * DI + (zct + 1) * 512],
                                          in_=ps[0:128, 0:512], func=AF.Silu)),
                            waits=[mm_info[(tb, "tok")], ssd_T_fence, ssd_V_fence, tok.get("sz_free")],
                            inc="C", tag=f"A1b{b}_zs{zct}_{tb}")
                ipps_ring.set_free(pslot, tz)
                tok[f"z{zct}_{tb}"] = tz
        tok["zw_free"] = S.now("T")
        if a1stop(f"b{b}z", [("sz", sz_sb[:]), ("y", y_sb[:])]):
            return True
        for tb in range(2):
            zw = [tok[f"z{zct}_{tb}"] for zct in range(8)]
            t_yz = S.emit("vector",
                          (lambda e, tb=tb: e.tensor_tensor(out=y_sb[:, tb * DI:(tb + 1) * DI],
                                                            in0=y_sb[:, tb * DI:(tb + 1) * DI],
                                                            in1=sz_sb[:, tb * DI:(tb + 1) * DI],
                                                            op=ALU.mult)),
                          waits=[ssd_V_fence] + zw, inc="V", tag=f"A1b{b}_yz{tb}")
            tok[f"y_free{tb}"] = t_yz
            t_sq = S.emit("scalar",
                          (lambda e, tb=tb: e.activation(out=ln_ring[:, 0:DI],
                                                         in_=y_sb[:, tb * DI:(tb + 1) * DI],
                                                         func=AF.Square)),
                          waits=[t_yz, tok.get("lnring_free")], inc="C", tag=f"A1b{b}_ysq{tb}")
            t_rd = S.emit("vector",
                          (lambda e, tb=tb: e.reduce_sum(
                              out=rstd_sb[:, tb * 64:(tb + 1) * 64],
                              in_=ln_ring[:, 0:DI].rearrange("p (h q) -> p h q", q=64), axis=AX.X)),
                          waits=[t_sq, tok.get(f"rstd_free{tb}")], inc="V", tag=f"A1b{b}_yrd{tb}")
            tok["lnring_free"] = t_rd

            t_rs0 = S.emit("vector",
                           (lambda e, tb=tb: e.tensor_scalar(out=rstd_sb[:, tb * 64:(tb + 1) * 64],
                                                             in0=rstd_sb[:, tb * 64:(tb + 1) * 64],
                                                             scalar1=1.0 / 64, scalar2=EPS_RMS,
                                                             op0=ALU.mult, op1=ALU.add)),
                           waits=[t_rd], inc="V", tag=f"A1b{b}_yrs{tb}")
            t_rc = S.emit("vector",
                          (lambda e, tb=tb: e.reciprocal(out=rstd_sb[:, tb * 64:(tb + 1) * 64],
                                                         in_=rstd_sb[:, tb * 64:(tb + 1) * 64])),
                          waits=[t_rs0], inc="V", tag=f"A1b{b}_yrc{tb}")
            t_rq = S.emit("scalar",
                          (lambda e, tb=tb: e.activation(out=rstd_sb[:, tb * 64:(tb + 1) * 64],
                                                         in_=rstd_sb[:, tb * 64:(tb + 1) * 64],
                                                         func=AF.Sqrt)),
                          waits=[t_rc], inc="C", tag=f"A1b{b}_yrq{tb}")
            tok[f"rstdq{tb}"] = t_rq
        sz_T_fence = S.now("V")
        nt_toks = []
        for tb in range(2):
            def fnm(e, tb=tb):
                return e.tensor_tensor(
                    out=y_sb[:, tb * DI:(tb + 1) * DI].rearrange("p (h q) -> p h q", q=64),
                    in0=y_sb[:, tb * DI:(tb + 1) * DI].rearrange("p (h q) -> p h q", q=64),
                    in1=rstd_sb[:, tb * 64:(tb + 1) * 64].rearrange("p (h o) -> p h o", o=1)
                    .broadcast_to([128, 64, 64]), op=ALU.mult)
            t_nm = S.emit("vector", fnm, waits=[tok[f"rstdq{tb}"], sz_T_fence], inc="V",
                          tag=f"A1b{b}_nm{tb}")
            tok[f"rstd_free{tb}"] = t_nm
            for round_ in range(4):
                pslot, pfr = lnp_ring.next()
                pb = lnps[pslot]
                last = None
                for q in range(8):
                    ct = round_ * 8 + q
                    last = S.emit("tensor",
                                  (lambda e, tb=tb, ct=ct, q=q, pb=pb:
                                   e.transpose(pb[:, q * 128:(q + 1) * 128],
                                               y_sb[:, tb * DI + ct * 128: tb * DI + (ct + 1) * 128],
                                               identb[:])),
                                  waits=[t_nm, pfr] if q == 0 else [t_nm], inc="T",
                                  tag=f"A1b{b}_ntp{tb}_{ct}")

                def fev(e, pb=pb, tb=tb, round_=round_):
                    r = None
                    for q in range(8):
                        ct = round_ * 8 + q
                        r = e.activation(out=normT[:, ct * 256 + tb * 128: ct * 256 + (tb + 1) * 128],
                                         in_=pb[:, q * 128:(q + 1) * 128], func=AF.Copy,
                                         scale=nw_fm[:, ct:ct + 1])
                    return r
                ev = S.emit("scalar", fev, waits=[last, tok["small"], sz_T_fence],
                            inc="C", tag=f"A1b{b}_nte{tb}_{round_}")
                lnp_ring.set_free(pslot, ev)
                nt_toks.append(ev)
        if a1stop(f"b{b}norm", [("normT", normT[:]), ("yn", y_sb[:])]):
            return True
        wob_ring = Ring(4)
        wob_ring.free = [tok["zw_free"]] * 4
        for tb in range(2):
            mm_info = {}
            for k in range(32):
                for half2 in range(2):
                    wslot, wfr = wob_ring.next()

                    def fl(e, k=k, wslot=wslot, half2=half2):
                        return e.dma_start(out=wob[:, wslot * 1024:(wslot + 1) * 1024],
                                           in_=woutb[k * 128:(k + 1) * 128,
                                                     half2 * 1024:(half2 + 1) * 1024])
                    tl = S.emit("sync", fl, waits=[wfr, tok["wcast"]], inc=f"dZ{wslot}", n=16,
                                tag=f"A1b{b}_ow{tb}_{k}_{half2}")
                    last = None
                    for jj in range(2):
                        j = half2 * 2 + jj
                        if k == 0:
                            pslot, pfr = ipps_ring.next()
                            mm_info[(j, "slot")] = pslot
                            wts = [tl, pfr] + nt_toks
                        else:
                            pslot = mm_info[(j, "slot")]
                            wts = [tl]
                        last = S.emit("tensor",
                                      (lambda e, k=k, tb=tb, jj=jj, wslot=wslot, ps=ip_ps[pslot]:
                                       e.matmul(ps[0:128, 0:512],
                                                normT[:, k * 256 + tb * 128:k * 256 + (tb + 1) * 128],
                                                wob[:, wslot * 1024 + jj * 512: wslot * 1024 + (jj + 1) * 512],
                                                start=(k == 0), stop=(k == 31))),
                                      waits=wts, inc="T", tag=f"A1b{b}_om{tb}_{j}_{k}")
                        mm_info[(j, "tok")] = last
                    wob_ring.set_free(wslot, last)
            if a1stop(f"b{b}om{tb}", [("normT", normT[:])]):
                return True
            mm4 = [mm_info[(j, "tok")] for j in range(4)]

            def fcp(e, mm_info=mm_info):
                r = None
                for j in range(4):
                    r = e.copy(out=ostage[:, j * 512:(j + 1) * 512],
                               in_=ip_ps[mm_info[(j, "slot")]][0:128, 0:512])
                return r
            t_cp = S.emit("scalar", fcp, waits=mm4 + [tok.get("ostats_free")], inc="C",
                          tag=f"A1b{b}_ocp{tb}")
            for j in range(4):
                ipps_ring.set_free(mm_info[(j, "slot")], t_cp)
            if a1stop(f"b{b}ocp{tb}", [("ost", ostage[:])]):
                return True

            def fred(e):
                r = None
                for j in range(4):
                    r = e.reduce_sum(out=stats[:, 28 + j:29 + j],
                                     in_=ostage[:, j * 512:(j + 1) * 512], axis=AX.X)
                return r
            t_red = S.emit("vector", fred, waits=[t_cp], inc="V", tag=f"A1b{b}_ord{tb}")
            if a1stop(f"b{b}ord{tb}", [("stats2", stats[:])]):
                return True

            t_sq = S.emit("scalar", lambda e: e.activation(
                out=gtrash[:, 0:2048], in_=ostage[:, 0:2048], func=AF.Square,
                accum_out=stats[:, 24:25]),
                waits=[t_cp], inc="C", tag=f"A1b{b}_osq{tb}")
            if a1stop(f"b{b}osq{tb}", [("stats2", stats[:]), ("ost", ostage[:])]):
                return True

            t_vA = S.emit("vector", lambda e: e.reduce_sum(
                out=stats[:, 20:21], in_=stats[:, 28:32], axis=AX.X),
                waits=[t_red, t_sq], inc="V", tag=f"A1b{b}_ovA{tb}")
            t_vC = S.emit("vector", lambda e: e.tensor_scalar_mul(
                out=stats[:, 20:21], in0=stats[:, 20:21], scalar1=1.0 / DM),
                waits=[t_vA], inc="V", tag=f"A1b{b}_ovC{tb}")
            t_vD = S.emit("vector", lambda e: e.tensor_tensor(
                out=stats[:, 22:23], in0=stats[:, 20:21], in1=stats[:, 20:21], op=ALU.mult),
                waits=[t_vC], inc="V", tag=f"A1b{b}_ovD{tb}")
            t_var = S.emit("vector", lambda e: e.scalar_tensor_tensor(
                out=stats[:, 21:22], in0=stats[:, 24:25], scalar=1.0 / DM,
                in1=stats[:, 22:23], op0=ALU.mult, op1=ALU.subtract),
                waits=[t_vD], inc="V", tag=f"A1b{b}_ova{tb}")
            t_sqr = S.emit("scalar", lambda e: e.activation(out=stats[:, 21:22], in_=stats[:, 21:22],
                                                            func=AF.Sqrt, bias=eps_sb[:, 0:1]),
                           waits=[t_var], inc="C", tag=f"A1b{b}_osr{tb}")
            t_rcp = S.emit("vector", lambda e: e.reciprocal(out=stats[:, 21:22], in_=stats[:, 21:22]),
                           waits=[t_sqr], inc="V", tag=f"A1b{b}_orc{tb}")
            if a1stop(f"b{b}orc{tb}", [("stats2", stats[:])]):
                return True
            gidx = 2 * b + tb

            t_stg = S.emit("vector",
                           (lambda e: e.tensor_scalar(
                               out=ostage[:, 0:DM], in0=ostage[:, 0:DM],
                               scalar1=stats[:, 20:21], scalar2=stats[:, 21:22],
                               op0=ALU.subtract, op1=ALU.mult)),
                           waits=[t_rcp], inc="V", tag=f"A1b{b}_ost{tb}")
            if a1stop(f"b{b}ostg{tb}", [("stats2", stats[:])]):
                return True
            t_res = S.emit("vector",
                           (lambda e, gidx=gidx: e.tensor_tensor(
                               out=x_res[:, gidx * DM:(gidx + 1) * DM],
                               in0=x_res[:, gidx * DM:(gidx + 1) * DM],
                               in1=ostage[:, 0:DM], op=ALU.add)),
                           waits=[t_stg], inc="V", tag=f"A1b{b}_ors{tb}")
            tok["ostats_free"] = t_res
            a1_res_toks.append(t_res)
            if a1stop(f"b{b}ores{tb}", [("xres_mid", x_res[:])]):
                return True
        tok["cpost_ok"] = S.now("T")
        tok["sz_free"] = S.now("T")
        tok["wob_zfree"] = S.now("T")
        if a1stop(f"b{b}done", [("xres_mid", x_res[:])]):
            return True

    tok["gprev_fence"] = S.now("C")
    tok["hq_done"] = S.now("dHQ")
    a1_stopped = False
    for _b in range(4):
        if emit_A1_block(_b):
            a1_stopped = True
            break
    if a1_stopped:
        return finish([])

    for nm, ap, w in [("y", y_sb[:], [a1_res_toks[-1]]), ("normT", normT[:], [a1_res_toks[-1]]),
                      ("xres_mid", x_res[:], a1_res_toks), ("cpost", cpost[:], [a1_res_toks[-1]]),
                      ("sz", sz_sb[:], [a1_res_toks[-1]])]:
        dbg_tap(nm, ap, w)

    if stop_after == "A1":
        return finish([tok["thr"]] + a1_res_toks)

    qw_ring = Ring(2)
    qw_ring.free = [tok["gamma2"]] * 2

    def emit_quant(widx, eng_name):
        w = [w1, w2, w3][widx]
        dst = [q1s, q2s, q3s][widx]
        inc_s = {"vector": "V", "gpsimd": "G"}[eng_name]
        pieces = []
        if widx < 2:
            for i in range(44):
                for j in range(4):
                    pieces.append((w[i * 128:(i + 1) * 128, j * 512:(j + 1) * 512],
                                   dst[i * 128:(i + 1) * 128, j * 512:(j + 1) * 512], 512))
        else:
            for i in range(16):
                for j in range(11):
                    pieces.append((w[i * 128:(i + 1) * 128, j * 512:(j + 1) * 512],
                                   dst[i * 128:(i + 1) * 128, j * 512:(j + 1) * 512], 512))
        last = None
        for pi, (src, dsts, wd) in enumerate(pieces):
            slot, fr = qw_ring.next()
            f32b = qw_f32[:, slot * 512: slot * 512 + wd]
            qab = qa_sb[:, slot * 512: slot * 512 + wd]
            tl = S.emit("sync", (lambda e, f32b=f32b, src=src: e.dma_start(out=f32b, in_=src)),
                        waits=[fr], inc=f"dQL{slot}", n=16, tag=f"q{widx}ld{pi}")
            t1 = S.emit(eng_name,
                        (lambda e, f32b=f32b, qab=qab, widx=widx:
                         e.tensor_single_scalar(out=qab, in_=f32b, scalar=thr_bc[:, 4 + widx:5 + widx],
                                                op=ALU.is_lt)),
                        waits=[tl, tok["thr"]], inc=inc_s, tag=f"q{widx}a{pi}")
            t2 = S.emit(eng_name,
                        (lambda e, f32b=f32b, qab=qab, widx=widx:
                         e.scalar_tensor_tensor(out=qab, in0=f32b, scalar=thr_bc[:, widx:widx + 1],
                                                in1=qab, op0=ALU.is_gt, op1=ALU.subtract)),
                        waits=[t1], inc=inc_s, tag=f"q{widx}b{pi}")
            t3 = S.emit("sync", (lambda e, qab=qab, dsts=dsts: e.dma_start(out=dsts, in_=qab)),
                        waits=[t2], inc=f"dQS{slot}", n=16, tag=f"q{widx}st{pi}")
            qw_ring.set_free(slot, t3)
            last = t3
        tok[f"q{widx}"] = last

    emit_quant(0, "vector")
    emit_quant(1, "vector")
    emit_quant(2, "vector")

    ln3_tiles = [(x_res[:, t * DM:(t + 1) * DM], a1_res_toks[t], None) for t in range(8)]
    tok["lnT_done_ffn"], _ = emit_ln_transpose("ln3", ln3_tiles, ln1T, stat_base=40,
                                           ring_init=[tok["lnring_free"]] * 2)

    f_es = ExitStack()
    esP.close()
    f_ps = [f_es.enter_context(nc.psum_tensor(f"f_ps{i}", [128, 512], F32)) for i in range(8)]
    aft = S.allnow()

    qt_ring = Ring(3)
    fps_ring = Ring(8)
    fps_ring.free = [aft[0]] * 8
    hb_toks = {}
    for jg in range(22):
        j0 = jg * 256
        mm_info = {}
        last = None
        for k in range(16):
            wslot, wfr = qt_ring.next()
            qb = qt_sb[:, wslot * 512:(wslot + 1) * 512]

            S.emit("scalar",
                   (lambda e, j0=j0, k=k, qb=qb:
                    e.dma_start(out=qb[:, 0:256], in_=q1s[j0:j0 + 256, k * 128:(k + 1) * 128],
                                transpose=True)),
                   waits=[wfr, tok["q0"], tok["q1"]], inc=f"dQT{wslot}", n=16, tag=f"f12w{jg}_{k}a")
            tl = S.emit("scalar",
                        (lambda e, j0=j0, k=k, qb=qb:
                         e.dma_start(out=qb[:, 256:512], in_=q2s[j0:j0 + 256, k * 128:(k + 1) * 128],
                                     transpose=True)),
                        waits=[], inc=f"dQT{wslot}", n=16, tag=f"f12w{jg}_{k}b")
            for jt in range(2):
                for th in range(2):
                    for xi in range(2):
                        pslot = jt * 2 + th + xi * 4
                        if k == 0:
                            _, pfr = fps_ring.next()
                            wts = [tl, tok["lnT_done_ffn"], pfr]
                        else:
                            wts = [tl]
                        last = S.emit("tensor",
                                      (lambda e, k=k, jt=jt, th=th, xi=xi, qb=qb, ps=f_ps[pslot]:
                                       e.matmul(ps[0:128, 0:512],
                                                qb[:, xi * 256 + jt * 128: xi * 256 + (jt + 1) * 128],
                                                ln1T[:, k * 1024 + th * 512: k * 1024 + (th + 1) * 512],
                                                start=(k == 0), stop=(k == 15))),
                                      waits=wts, inc="T", tag=f"f12m{jg}_{jt}{th}{xi}_{k}")
                        mm_info[(jt, th, xi)] = last
            qt_ring.set_free(wslot, last)
        for jt in range(2):
            for th in range(2):
                sgs = sg_sb[:, th * 512:(th + 1) * 512]
                t_sg = S.emit("scalar",
                              (lambda e, jt=jt, th=th, sgs=sgs:
                               e.activation(out=sgs, in_=f_ps[jt * 2 + th][0:128, 0:512], func=AF.Sigmoid)),
                              waits=[mm_info[(jt, th, 0)], tok.get(f"sg_free{th}")], inc="C",
                              tag=f"f12s{jg}_{jt}{th}")

                def fh(e, jg=jg, jt=jt, th=th, sgs=sgs):
                    return e.tensor_tensor(out=hid_sb[:, (jg * 2 + jt) * TOK + th * 512:
                                                      (jg * 2 + jt) * TOK + (th + 1) * 512],
                                           in0=f_ps[jt * 2 + th + 4][0:128, 0:512], in1=sgs, op=ALU.mult)
                t_h = S.emit("vector", fh, waits=[t_sg, mm_info[(jt, th, 1)]], inc="V",
                             tag=f"f12h{jg}_{jt}{th}")
                tok[f"sg_free{th}"] = t_h
                fps_ring.set_free(jt * 2 + th, t_h)
                fps_ring.set_free(jt * 2 + th + 4, t_h)
                hb_toks[(jg * 2 + jt, th)] = t_h
    hid_done = [hb_toks[(j, th)] for j in range(44) for th in range(2)]
    hid_fence = S.emit("vector", lambda e: e.memset(stats[0:1, 30:31], 0.0),
                       waits=[hb_toks[(43, 0)], hb_toks[(43, 1)], S.now("V")], inc="V", tag="hidf")

    q3r = Ring(2)
    q3r.free = [hid_fence] * 2
    f3_free = {t: hid_fence for t in range(8)}
    ffo_toks = [None] * 8
    for p in range(4):
        mm_info = {}
        for k in range(44):
            wslot, wfr = q3r.next()

            def fl(e, p=p, k=k, wslot=wslot):
                return e.dma_start(out=q3sl[:, wslot * 512:(wslot + 1) * 512],
                                   in_=q3s[p * 512:(p + 1) * 512, k * 128:(k + 1) * 128], transpose=True)
            tl = S.emit("scalar", fl, waits=[wfr, tok["q2"]], inc=f"dQ3{wslot}", n=16, tag=f"f3w{p}_{k}")
            last = None
            for t in range(8):
                wts = [tl, hid_fence]
                if k == 0:
                    wts.append(f3_free[t])
                last = S.emit("tensor",
                              (lambda e, k=k, t=t, wslot=wslot, ps=f_ps[t]:
                               e.matmul(ps[0:128, 0:512],
                                        hid_sb[:, k * TOK + t * 128:k * TOK + (t + 1) * 128],
                                        q3sl[:, wslot * 512:(wslot + 1) * 512],
                                        start=(k == 0), stop=(k == 43))),
                              waits=wts, inc="T", tag=f"f3m{p}_{t}_{k}")
                mm_info[t] = last
            q3r.set_free(wslot, last)
        for t in range(8):
            ev = S.emit("scalar",
                        (lambda e, p=p, t=t, ps=f_ps[t]:
                         e.copy(out=ffn_o[:, t * 2048 + p * 512: t * 2048 + (p + 1) * 512],
                                in_=ps[0:128, 0:512])),
                        waits=[mm_info[t], tok["lnT_done_ffn"]], inc="C", tag=f"f3e{p}_{t}")
            f3_free[t] = ev
            if p == 3:
                ffo_toks[t] = ev

    st_final = []
    for t in range(8):
        src = ffn_o[:, t * 2048:(t + 1) * 2048]
        t_red = S.emit("vector", (lambda e, src=src: e.reduce_sum(out=stats[:, 0:1], in_=src, axis=AX.X)),
                       waits=[ffo_toks[t], tok.get("lnst_free")], inc="V", tag=f"ln4_sum{t}")
        t_sq = S.emit("scalar", (lambda e, src=src: e.activation(out=gtrash[:, 0:2048], in_=src,
                                                                 func=AF.Square, accum_out=stats[:, 1:2])),
                      waits=[ffo_toks[t], tok.get("lnst_free")], inc="C", tag=f"ln4_sq{t}")

        t_v1 = S.emit("vector", lambda e: e.tensor_scalar_mul(
            out=stats[:, 2:3], in0=stats[:, 0:1], scalar1=1.0 / DM),
            waits=[t_red, t_sq], inc="V", tag=f"ln4_vm{t}")
        t_v2 = S.emit("vector", lambda e: e.tensor_tensor(
            out=stats[:, 3:4], in0=stats[:, 2:3], in1=stats[:, 2:3], op=ALU.mult),
            waits=[t_v1], inc="V", tag=f"ln4_vq{t}")
        t_var = S.emit("vector", lambda e: e.scalar_tensor_tensor(
            out=stats[:, 3:4], in0=stats[:, 1:2], scalar=1.0 / DM,
            in1=stats[:, 3:4], op0=ALU.mult, op1=ALU.subtract),
            waits=[t_v2], inc="V", tag=f"ln4_var{t}")
        t_sqr = S.emit("scalar", lambda e: e.activation(out=stats[:, 3:4], in_=stats[:, 3:4],
                                                        func=AF.Sqrt, bias=eps_sb[:, 0:1]),
                       waits=[t_var], inc="C", tag=f"ln4_sqt{t}")

        t_rc4 = S.emit("vector", lambda e: e.reciprocal(out=stats[:, 3:4], in_=stats[:, 3:4]),
                       waits=[t_sqr], inc="V", tag=f"ln4_rcp{t}")

        t_oa = S.emit("vector",
                      (lambda e, src=src: e.tensor_scalar(
                          out=ostage[:], in0=src, scalar1=stats[:, 2:3], scalar2=stats[:, 3:4],
                          op0=ALU.subtract, op1=ALU.mult)),
                      waits=[t_rc4, tok.get("ost_free")], inc="V", tag=f"ln4_oa{t}")
        t_o = S.emit("vector",
                     (lambda e, t=t: e.tensor_tensor(
                         out=ostage[:], in0=ostage[:],
                         in1=x_res[:, t * DM:(t + 1) * DM], op=ALU.add)),
                     waits=[t_oa], inc="V", tag=f"ln4_o{t}")
        tok["lnst_free"] = t_o
        t_st = S.emit("sync", (lambda e, t=t: e.dma_start(out=out[t * 128:(t + 1) * 128, :],
                                                          in_=ostage[:])),
                      waits=[t_o], inc="ST", n=16, tag=f"out{t}")
        tok["ost_free"] = t_st
        st_final.append(t_st)

    for nm, ap in [("thr", thr_bc[:]), ("hid", hid_sb[:]), ("ffo", ffn_o[:]),
                   ("ln3T", ln1T[:])]:
        dbg_tap(nm, ap, S.allnow())
    nc_, dbg_ = finish(st_final)
    f_es.close()
    return nc_, dbg_


def kernel(**inputs):
    nc, _ = build_nc()
    u_full = np.ascontiguousarray(inputs["L0_attn_u"], dtype=np.float32)
    zeros = np.zeros((TOK, DM), np.float32)
    in_maps = []
    for core in range(8):
        seq, half = core // 2, core % 2
        own = np.ascontiguousarray(u_full[seq, half * TOK:(half + 1) * TOK, :])
        ctx = np.ascontiguousarray(u_full[seq, 0:TOK, :]) if half else zeros
        in_maps.append({
            "u": own, "ucx": ctx,
            "W_in": inputs["L0_attn_W_in"], "conv_w": inputs["L0_attn_conv_w"],
            "conv_b": inputs["L0_attn_conv_b"], "dt_bias": inputs["L0_attn_dt_bias"],
            "A_log": inputs["L0_attn_A_log"], "Dp": inputs["L0_attn_D"],
            "norm_w": inputs["L0_attn_norm_weight"], "W_out": inputs["L0_attn_W_out"],
            "w1": inputs["L0_ffn_w1"], "w2": inputs["L0_ffn_w2"], "w3": inputs["L0_ffn_w3"],
        })
    res = run_bass_kernel_spmd(nc, in_maps, core_ids=list(range(8)))
    outp = np.empty((4, 2048, DM), np.float32)
    for core in range(8):
        seq, half = core // 2, core % 2
        outp[seq, half * TOK:(half + 1) * TOK, :] = res.results[core]["out"]
    return outp


if __name__ == "__main__":
    import sys
    sa = sys.argv[1] if len(sys.argv) > 1 else "A0"
    nc, dbg = build_nc(dbg=(), stop_after=None if sa == "full" else sa)
    print(f"built {sa} ok, instructions: {nc.n_instructions()}")
